# version 30
# baseline (speedup 1.0000x reference)
import numpy as np
from contextlib import ExitStack

import concourse.bass as bass
import concourse.mybir as mybir
from concourse.bass_utils import run_bass_kernel_spmd

F32 = mybir.dt.float32
BF16 = mybir.dt.bfloat16
AF = mybir.ActivationFunctionType
OP = mybir.AluOpType
AX = mybir.AxisListType

B, L, D, NST, E, HID, R = 8, 2048, 1024, 16, 8, 2048, 16
TC = 512
NTC = L // TC
DB = D // 128
HB = HID // 128
TB = TC // 128
EPS = 1e-5
SERIAL = False

SEMS = ("ld", "st", "wq", "lw", "pe", "dve", "act", "gp")
SEM_ENG = {"pe": "tensor", "dve": "vector", "act": "scalar", "gp": "gpsimd"}

W13_SLOTS = 5
W2_SLOTS = 3
WDT_SLOTS = 2

DEBUG = ()


class Cx:
    def __init__(self, active, eng, sems, trace=None):
        self.active = active
        self.eng = eng
        self.sems = sems
        self.cnt = {s: 0 for s in SEMS}
        self.trace = trace

    def do(self, engname, fn, waits=(), inc=None, kind=""):
        waits = [(s, v) for (s, v) in waits
                 if v > 0 and SEM_ENG.get(s) != engname]
        if SERIAL:
            extra = [(s, self.cnt[s]) for s in SEMS
                     if self.cnt[s] > 0 and SEM_ENG.get(s) != engname]
            seen = {s for s, _ in waits}
            waits = waits + [(s, v) for (s, v) in extra if s not in seen]
        if self.trace is not None:
            self.trace.append((engname, tuple(waits),
                               (inc,) if inc else (), kind))
        if self.active == engname:
            for s, v in waits:
                self.eng.wait_ge(self.sems[s], v)
            r = fn(self.eng)
            if inc is not None:
                r.then_inc(self.sems[inc[0]], inc[1])
        if inc is not None:
            self.cnt[inc[0]] += inc[1]
        return self.cnt[inc[0]] if inc else None

    def waitonly(self, engname, waits):
        waits = [(s, v) for (s, v) in waits if v > 0]
        if self.trace is not None:
            self.trace.append((engname, tuple(waits), (), "wait"))
        if self.active == engname:
            for s, v in waits:
                self.eng.wait_ge(self.sems[s], v)

    def c(self, s):
        return self.cnt[s]


def _simulate(trace):
    engs = {}
    for op in trace:
        engs.setdefault(op[0], []).append(op)
    pcs = {e: 0 for e in engs}
    cnt = {s: 0 for s in SEMS}
    total = sum(len(v) for v in engs.values())
    done = 0
    while done < total:
        progressed = False
        for e, ops in engs.items():
            while pcs[e] < len(ops):
                _, waits, incs, kind = ops[pcs[e]]
                if any(cnt[s] < v for s, v in waits):
                    break
                for inc in incs:
                    cnt[inc[0]] += inc[1]
                pcs[e] += 1
                done += 1
                progressed = True
        if not progressed:
            lines = []
            for e, ops in engs.items():
                if pcs[e] < len(ops):
                    _, waits, _, kind = ops[pcs[e]]
                    blocked = [(s, v, cnt[s]) for s, v in waits if cnt[s] < v]
                    lines.append(f"{e} pc={pcs[e]} kind={kind} blocked on {blocked}")
            raise RuntimeError("DEADLOCK in schedule:\n" + "\n".join(lines))


def build_nc():
    nc = bass.Bass("TRN2", target_bir_lowering=False)

    dr = {}

    def din(name, shape, dt=F32):
        dr[name] = nc.dram_tensor(name, shape, dt, kind="ExternalInput").ap()
        return dr[name]

    din("xin", [NTC, 128, DB * TC])
    dr["out"] = nc.dram_tensor("out", [NTC, 128, DB * TC], F32,
                               kind="ExternalOutput").ap()
    din("w13t", [E, HB, 2, 128, 1024])
    din("w2t", [E, DB, 128, HB * 128])
    din("wdt", [DB, 128, DB * 128])
    din("wbt", [128, DB * NST])
    din("wgt", [128, DB * E])
    din("a13t", [E, 128, DB * 64])
    din("b13t", [E, 64, HID])
    din("a2t", [E, 128, HB * 32])
    din("b2t", [E, 32, D])
    din("b2v", [E, D])
    din("b1r", [128, E * HB])
    din("b3r", [128, E * HB])
    din("amat", [128, DB * NST])
    din("cmat", [128, DB * NST])
    din("dpm", [128, DB])
    din("bdm", [128, DB])
    din("sel16", [16, NST * 128])
    din("sel8", [8, E * 128])
    din("ident", [128, 128])
    dr["w13bf"] = nc.dram_tensor("w13bf", [E, HB, 128, 2048], BF16,
                                 kind="Internal").ap()
    dr["w2bf"] = nc.dram_tensor("w2bf", [E, DB, 128, 2048], BF16,
                                 kind="Internal").ap()
    _dbg_shapes = {"ubf": ([128, DB * TC], BF16), "dt": ([128, DB * TC], F32),
                   "bx": ([16, 2 * TC], BF16), "x2": ([128, DB * TC], F32),
                   "v": ([128, DB * TC], BF16), "gsc": ([128, 32], F32),
                   "gt": ([8, 2 * TC], BF16), "g": ([128, HB * TC], BF16),
                   "y0": ([128, DB * TC], F32), "q13": ([64, 2 * TC], BF16),
                   "gbc": ([128, 2 * TC], BF16)}
    for k in DEBUG:
        shp, dty = _dbg_shapes[k]
        dr["dbg_" + k] = nc.dram_tensor("dbg_" + k, shp, dty,
                                        kind="ExternalOutput").ap()

    with ExitStack() as ctx:
        t = {}

        def sb(name, shape, dt=F32):
            t[name] = ctx.enter_context(nc.sbuf_tensor("s_" + name, shape, dt))
            return t[name]

        def psum(name):
            t[name] = ctx.enter_context(nc.psum_tensor(name, [128, 512], F32))
            return t[name]

        sb("x0", [128, DB * TC]); sb("x1", [128, DB * TC])
        sb("ubf", [128, DB * TC], BF16)
        sb("vbf0", [128, DB * TC], BF16); sb("vbf1", [128, DB * TC], BF16)
        sb("dt", [128, DB * TC])
        sb("sq", [128, 2 * TC], BF16)
        sb("sa", [128, 4 * TC]); sb("sbb", [128, 2 * TC]); sb("sh", [128, 2 * TC])
        sb("carry", [128, 128])
        sb("bx", [16, 2 * TC], BF16)
        sb("g0", [128, HB * TC], BF16); sb("g1", [128, HB * TC], BF16)
        sb("silu", [128, 2 * TC], BF16)
        sb("t1", [128, 2 * TC], BF16)
        sb("q13", [64, 2 * TC], BF16)
        sb("q2", [32, 2 * TC], BF16)
        sb("gbc", [128, 2 * TC], BF16)
        sb("gt", [8, 2 * TC], BF16)
        sb("gsc", [128, 32]); sb("m8", [128, 32]); sb("msk", [128, 32])
        sb("ge", [128, 32]); sb("zz", [128, 4]); sb("rz", [128, 4])
        sb("junk", [128, 8])
        sb("stat", [1, 3 * TC])
        sb("epsr", [1, 1])
        sb("w13", [128, W13_SLOTS * 2048], BF16)
        sb("w2r", [128, W2_SLOTS * 2048], BF16)
        sb("wdtr", [128, WDT_SLOTS * 1024], BF16)
        sb("a13r", [128, 2 * DB * 64], BF16)
        sb("a2r", [128, 2 * HB * 32], BF16)
        sb("b13r", [64, 2 * HID], BF16)
        sb("b2r", [32, 2 * D], BF16)
        sb("wbt", [128, DB * NST], BF16)
        sb("wgt", [128, DB * E], BF16)

        sb("b2v", [E, D], BF16)
        sb("b1r", [128, E * HB]); sb("b3r", [128, E * HB])
        sb("amat", [128, DB * NST]); sb("cmat", [128, DB * NST])
        sb("dpm", [128, DB]); sb("bdm", [128, DB])
        sb("sel16", [16, NST * 128], BF16); sb("sel8", [8, E * 128], BF16)
        sb("ident", [128, 128])
        sb("onesc", [128, 1]); sb("onesb", [128, 1], BF16); sb("onesr", [1, 128])

        for i in range(8):
            psum("p%d" % i)

        sems = {s: ctx.enter_context(nc.semaphore(s)) for s in SEMS}
        block = ctx.enter_context(nc.Block())

        trace = []
        cx0 = Cx(None, None, sems, trace=trace)
        emit_all(cx0, t, dr)
        _simulate(trace)

        def mk(engname):
            def closure(eng):
                emit_all(Cx(engname, eng, sems), t, dr)
            return closure

        block.sync(mk("sync"))
        block.gpsimd(mk("gpsimd"))
        block.tensor(mk("tensor"))
        block.vector(mk("vector"))
        block.scalar(mk("scalar"))

    return nc


def emit_all(cx, t, dr):
    bank_free = {i: None for i in range(8)}

    def bank_wait(i):
        return [bank_free[i]] if bank_free[i] else []

    def dma(q, fn, waits=(), kind=""):
        sem = {"sync": "ld", "gpsimd": "wq", "scalar": "lw"}[q]
        cx.do(q, fn, waits=waits, inc=(sem, 16), kind=kind)
        return cx.c(sem)

    def store(fn, waits=(), kind="store"):
        cx.do("sync", fn, waits=waits, inc=("st", 16), kind=kind)
        return cx.c("st")

    def vec(fn, waits=(), kind=""):
        cx.do("vector", fn, waits=waits, inc=("dve", 1), kind=kind)
        return cx.c("dve")

    def act(fn, waits=(), kind=""):
        cx.do("scalar", fn, waits=waits, inc=("act", 1), kind=kind)
        return cx.c("act")

    def gp(fn, waits=(), kind=""):
        cx.do("gpsimd", fn, waits=waits, inc=("gp", 1), kind=kind)
        return cx.c("gp")

    def pe_group(fns, waits=(), kind="", late=None):
        for i, f in enumerate(fns):
            w = waits if i == 0 else ()
            if late and i in late:
                w = list(w) + late[i]
            cx.do("tensor", f, waits=w,
                  inc=("pe", 1) if i == len(fns) - 1 else None, kind=kind)
        return cx.c("pe")

    def dbg_tap(key, buf_name, waits):
        if key in DEBUG:
            store(lambda e: e.dma_start(out=dr["dbg_" + key],
                                        in_=t[buf_name][:]),
                  waits=waits, kind="dbg:" + key)

    res_names = ["b1r", "b3r", "amat", "cmat", "dpm", "bdm", "ident"]
    for nm in res_names:
        dma("sync", lambda e, nm=nm: e.dma_start(out=t[nm][:], in_=dr[nm][:]),
            kind="res:" + nm)
    ld_res = cx.c("ld")
    for nm in ["wbt", "wgt", "b2v", "sel16", "sel8"]:
        dma("gpsimd", lambda e, nm=nm: e.dma_start(out=t[nm][:], in_=dr[nm][:]),
            kind="resbf:" + nm)
    wq_res = cx.c("wq")

    vec(lambda e: e.memset(t["onesc"][:], 1.0), kind="ones")
    vec(lambda e: e.memset(t["onesb"][:], 1.0), kind="ones")
    vec(lambda e: e.memset(t["onesr"][:], 1.0), kind="ones")
    vec(lambda e: e.memset(t["junk"][:], 0.0), kind="ones")
    vec(lambda e: e.memset(t["carry"][:], 0.0), kind="ones")
    ev_ones = vec(lambda e: e.memset(t["epsr"][:], EPS), kind="ones")

    slot_rel = {}

    def stream(nm_slot, fn, kind, q="gpsimd"):
        w = slot_rel.get(nm_slot) or []
        if w and not isinstance(w, list):
            w = [w]
        return dma(q, fn, waits=w, kind=kind)

    def ln_stats(xb, dst_ev):
        S = t["stat"]
        mean_s = S[0:1, 0:TC]
        ms_s = S[0:1, TC:2 * TC]
        rstd_s = S[0:1, 2 * TC:3 * TC]
        w0 = bank_wait(0) + dst_ev + [("dve", ev_ones)]
        pe_mean = pe_group([
            (lambda e, d=d, xb=xb: e.matmul(
                t["p0"][0:1, :], t["onesc"][:, :],
                t[xb][:, d * TC:(d + 1) * TC],
                start=(d == 0), stop=(d == DB - 1)))
            for d in range(DB)], waits=w0, kind="ln:meanmm")
        w1 = bank_wait(1)
        pe_half = None
        for half in range(4):
            sq_ev = None
            for j in range(2):
                d = half * 2 + j
                wsq = list(dst_ev) if (half == 0 and j == 0) else []
                if half >= 1 and j == 0 and pe_half:
                    wsq.append(("pe", pe_half))
                sq_ev = act(lambda e, d=d, j=j, xb=xb: e.activation(
                    t["sq"][:, j * TC:(j + 1) * TC],
                    t[xb][:, d * TC:(d + 1) * TC],
                    AF.Square, bias=0.0, scale=1.0),
                    waits=wsq, kind="ln:sq")
            wmm = (w1 if half == 0 else []) + [("act", sq_ev),
                                               ("dve", ev_ones)]
            pe_half = pe_group([
                (lambda e, j=j, half=half: e.matmul(
                    t["p1"][0:1, :], t["onesb"][:, :],
                    t["sq"][:, j * TC:(j + 1) * TC],
                    start=(half == 0 and j == 0),
                    stop=(half == 3 and j == 1)))
                for j in range(2)], waits=wmm, kind="ln:sqmm")
        pe_sq = pe_half
        vec(lambda e: e.tensor_scalar_mul(mean_s, t["p0"][0:1, :], 1.0 / D),
            waits=[("pe", pe_mean)], kind="ln:mean")
        vec(lambda e: e.tensor_scalar_mul(ms_s, t["p1"][0:1, :], 1.0 / D),
            waits=[("pe", pe_sq)], kind="ln:ms")
        vec(lambda e: e.tensor_mul(rstd_s, mean_s, mean_s), kind="ln:m2")
        ev_var = vec(lambda e: e.tensor_sub(ms_s, ms_s, rstd_s), kind="ln:var")
        ev_shift = vec(lambda e: e.tensor_scalar_mul(mean_s, mean_s, -1.0),
                       kind="ln:shift")
        ev_sd = act(lambda e: e.activation(rstd_s, ms_s, AF.Sqrt,
                                           bias=t["epsr"][0:1, 0:1], scale=1.0),
                    waits=[("dve", ev_shift)], kind="ln:sd")
        ev_rstd = vec(lambda e: e.reciprocal(rstd_s, rstd_s),
                      waits=[("act", ev_sd)], kind="ln:rstd")
        pe_shift = pe_group([lambda e: e.matmul(
            t["p0"][:, :], t["onesr"][:, :], mean_s, start=True, stop=True)],
            waits=[("dve", ev_shift)], kind="ln:shiftbc")
        pe_rstd = pe_group([lambda e: e.matmul(
            t["p1"][:, :], t["onesr"][:, :], rstd_s, start=True, stop=True)],
            waits=[("dve", ev_rstd)], kind="ln:rstdbc")
        return (pe_shift, pe_rstd)

    def ln_norm(xb, ds, bc, consume):
        pe_shift, pe_rstd = bc
        usl = t["sh"][:, 0:TC]
        for d in ds:
            w = []
            if d == ds[0]:
                w = [("pe", pe_shift), ("pe", pe_rstd)]
                if S.get("carry_src") is not None:
                    w.append(S["carry_src"])
            vec(lambda e, d=d, xb=xb: e.tensor_add(
                usl, t[xb][:, d * TC:(d + 1) * TC], t["p0"][:, :]),
                waits=w, kind="ln:sub")
            vec(lambda e: e.tensor_mul(usl, usl, t["p1"][:, :]),
                kind="ln:mul")
            consume(d)
        last = cx.c("dve")
        if ds[-1] == DB - 1:
            bank_free[0] = ("dve", last)
            bank_free[1] = ("dve", last)
        return last

    S = dict(ev_dt=None, scan_guard=None, ev_store={}, ev_scan_done={},
             ev_vbf={}, ev_gt={}, ev_ubf={}, scan_hist={}, scr={},
             scr_pend={}, sh_free={0: None, 1: None}, carry_src=None)
    last_gmul = {0: None, 1: None}

    def sp1(p): return t["sa"][:, (2 * p) * TC:(2 * p + 1) * TC]
    def sp2(p): return t["sa"][:, (2 * p + 1) * TC:(2 * p + 2) * TC]
    def sp3(p): return t["sbb"][:, p * TC:(p + 1) * TC]

    def ph_ln1_stats(tc):
        par = tc % 2
        xb = "x%d" % par
        wx = [("st", S["ev_store"][tc - 2])] if tc >= 2 else []
        ev_xld = dma("sync", lambda e, tc=tc, xb=xb: e.dma_start(
            out=t[xb][:], in_=dr["xin"][tc]), waits=wx, kind="xload")
        S["ln_bc"] = ln_stats(xb, [("ld", ev_xld)])

    def ph_ln1_norm(tc, half):
        par = tc % 2
        xb = "x%d" % par

        def consume_ln1(d, xb=xb):
            vec(lambda e, d=d: e.tensor_copy(
                t["ubf"][:, d * TC:(d + 1) * TC], t["sh"][:, 0:TC]),
                kind="ubf")
            vec(lambda e, d=d, xb=xb: e.scalar_tensor_tensor(
                t[xb][:, d * TC:(d + 1) * TC], t["sh"][:, 0:TC],
                t["dpm"][:, d:d + 1],
                t[xb][:, d * TC:(d + 1) * TC], OP.mult, OP.add), kind="dp")
        ds = list(range(half * 4, half * 4 + 4))
        ln_norm(xb, ds, S["ln_bc"], consume_ln1)
        if half == 1:
            S["ev_ubf"][tc] = cx.c("dve")
            if tc == 0:
                dbg_tap("ubf", "ubf", [("dve", S["ev_ubf"][tc])])

    def ph_dt_eb(tc, eb):
        ev_ubf = S["ev_ubf"][tc]

        def wdt_stream(eb2):
            slot = eb2 % WDT_SLOTS
            return stream(("wdt", slot), lambda e, eb2=eb2, slot=slot:
                e.dma_start(out=t["wdtr"][:, slot * 1024:(slot + 1) * 1024],
                            in_=dr["wdt"][eb2]), kind="wdt")
        if eb == 0:
            S["wdt_dma"] = [wdt_stream(e2) for e2 in range(WDT_SLOTS)]
            S["ev_dt_par"] = {0: None, 1: None}
        wdt_dma = S["wdt_dma"]
        ev_dt_par = S["ev_dt_par"]
        if True:
            slot = eb % WDT_SLOTS
            bk = eb % 2
            pp = eb % 2
            wts = bank_wait(bk) + [("wq", wdt_dma[eb]), ("dve", ev_ubf)]
            pm = pe_group([
                (lambda e, d=d, slot=slot, bk=bk: e.matmul(
                    t["p%d" % bk][:, :],
                    t["wdtr"][:, slot * 1024 + d * 128: slot * 1024 + (d + 1) * 128],
                    t["ubf"][:, d * TC:(d + 1) * TC],
                    start=(d == 0), stop=(d == DB - 1)))
                for d in range(DB)], waits=wts, kind="dtmm")
            slot_rel[("wdt", slot)] = [("pe", pm)]
            if eb + WDT_SLOTS < DB:
                wdt_dma.append(wdt_stream(eb + WDT_SLOTS))
            wsp = [("pe", pm)]
            if eb < 2 and S["scan_guard"] is not None:
                wsp.append(("dve", S["scan_guard"]))
            elif ev_dt_par[pp] is not None:
                wsp.append(("dve", ev_dt_par[pp]))
            a1 = act(lambda e, eb=eb, bk=bk, pp=pp: e.activation(
                sp1(pp), t["p%d" % bk][:, :], AF.Relu,
                bias=t["bdm"][:, eb:eb + 1], scale=1.0),
                waits=wsp, kind="sp:relu")
            a2_ = act(lambda e, eb=eb, bk=bk, pp=pp: e.activation(
                sp2(pp), t["p%d" % bk][:, :], AF.Abs,
                bias=t["bdm"][:, eb:eb + 1], scale=1.0), kind="sp:abs")
            bank_free[bk] = ("act", a2_)
            act(lambda e, pp=pp: e.activation(sp3(pp), sp2(pp), AF.Exp,
                                              bias=0.0, scale=-1.0),
                kind="sp:exp")
            a4 = act(lambda e, pp=pp: e.activation(sp2(pp), sp3(pp), AF.Ln,
                                                   bias=1.0, scale=1.0),
                     kind="sp:ln")
            S["ev_dt"] = vec(lambda e, eb=eb, pp=pp: e.tensor_add(
                t["dt"][:, eb * TC:(eb + 1) * TC], sp1(pp), sp2(pp)),
                waits=[("act", a4)], kind="sp:add")
            ev_dt_par[pp] = S["ev_dt"]

    def ph_bx(tc):
        par = tc % 2
        ev_ubf = S["ev_ubf"][tc]
        if tc == 0:
            dbg_tap("dt", "dt", [("dve", S["ev_dt"])])
        wbx = bank_wait(0) + [("wq", wq_res), ("dve", ev_ubf)]
        pbx = pe_group([
            (lambda e, d=d: e.matmul(
                t["p0"][0:16, :], t["wbt"][:, d * 16:(d + 1) * 16],
                t["ubf"][:, d * TC:(d + 1) * TC],
                start=(d == 0), stop=(d == DB - 1)))
            for d in range(DB)], waits=wbx, kind="bxmm")
        S["ev_bx"] = vec(lambda e, par=par: e.tensor_copy(
            t["bx"][0:16, par * TC:(par + 1) * TC], t["p0"][0:16, :]),
            waits=[("pe", pbx)], kind="bxevac")
        bank_free[0] = ("dve", S["ev_bx"])

    def ph_scan_pair(tc, n, p):
        par = tc % 2
        xb = "x%d" % par
        bk = n % 2
        d0 = 2 * p
        d1 = d0 + 1
        if p == 0:
            wbc = bank_wait(bk) + [("dve", S["ev_bx"])]
            S["pbc"] = pe_group([lambda e, n=n, bk=bk, par=par: e.matmul(
                t["p%d" % bk][:, :], t["sel16"][:, n * 128:(n + 1) * 128],
                t["bx"][0:16, par * TC:(par + 1) * TC],
                start=True, stop=True)], waits=wbc, kind="bxbc")
        pbc = S["pbc"]
        u = n * 4 + p
        hp = u % 2
        asl = slice(hp * 2 * TC, (hp + 1) * 2 * TC)
        if u < 2:
            wexp = [("dve", S["ev_dt"])]
        else:
            wexp = [("dve", S["scan_hist"][u - 2])]
        for j, d in enumerate((d0, d1)):
            aev = act(lambda e, d=d, n=n, hp=hp, j=j: e.activation(
                t["sa"][:, (hp * 2 + j) * TC:(hp * 2 + j + 1) * TC],
                t["dt"][:, d * TC:(d + 1) * TC], AF.Exp,
                bias=0.0, scale=t["amat"][:, d * 16 + n:d * 16 + n + 1]),
                waits=wexp if j == 0 else (), kind="scan:exp")
        last_stt = None
        for j, d in enumerate((d0, d1)):
            last_stt = vec(lambda e, d=d, n=n, j=j, bk=bk:
                e.scalar_tensor_tensor(
                    t["sbb"][:, j * TC:(j + 1) * TC],
                    t["dt"][:, d * TC:(d + 1) * TC],
                    t["cmat"][:, d * 16 + n:d * 16 + n + 1],
                    t["p%d" % bk][:, :], OP.mult, OP.mult),
                waits=[("pe", pbc)] if (p == 0 and j == 0) else (),
                kind="scan:stt")
        if p == 3:
            bank_free[bk] = ("dve", last_stt)
        acol = t["sa"][:, (hp * 2 + 1) * TC:(hp * 2 + 1) * TC + 1]
        vec(lambda e, d1=d1, n=n: e.tensor_mul(
            t["junk"][:, 0:1], acol,
            t["carry"][:, d1 * 16 + n:d1 * 16 + n + 1]),
            waits=[("act", aev)], kind="scan:f1")
        vec(lambda e: e.memset(acol, 0.0), kind="scan:f3")
        vec(lambda e: e.tensor_add(
            t["sbb"][:, TC:TC + 1], t["sbb"][:, TC:TC + 1],
            t["junk"][:, 0:1]), kind="scan:f2")
        wsc = []
        if S["sh_free"][0] is not None:
            wsc.append(S["sh_free"][0])
        if S["carry_src"] is not None and u < DB:
            wsc.append(S["carry_src"])
        ev_scan = vec(lambda e, d0=d0, n=n, asl=asl: e.tensor_tensor_scan(
            t["sh"][:, 0:2 * TC], t["sa"][:, asl], t["sbb"][:, 0:2 * TC],
            t["carry"][:, d0 * 16 + n:d0 * 16 + n + 1],
            OP.mult, OP.add),
            waits=wsc, kind="scan:scan")
        S["scan_hist"][u] = ev_scan
        vec(lambda e, d0=d0: e.tensor_add(
            t[xb][:, d0 * TC:(d0 + 2) * TC],
            t[xb][:, d0 * TC:(d0 + 2) * TC], t["sh"][:, 0:2 * TC]),
            kind="scan:add")
        cev = vec(lambda e, d0=d0, d1=d1, n=n: e.tensor_copy(
            bass.AP(t["carry"], d0 * 16 + n, [[128, 128], [16, 2]]),
            bass.AP(t["sh"], TC - 1, [[2 * TC, 128], [TC, 2]])),
            kind="scan:carry")
        S["scan_guard"] = ev_scan
        S["sh_free"][0] = None
        if n == NST - 1 and p == 3:
            done = [("dve", cx.c("dve"))]
            S["carry_src"] = ("dve", cx.c("dve"))
            S["ev_scan_done"][tc] = done
            if tc == 0:
                dbg_tap("x2", xb, done)

    def ph_ln2_stats(tc):
        xb = "x%d" % (tc % 2)
        S["ln_bc"] = ln_stats(xb, list(S["ev_scan_done"][tc]))

    def ph_ln2_norm(tc, half):
        par = tc % 2
        xb = "x%d" % par
        vb = "vbf%d" % par

        def consume_ln2(d, vb=vb):
            vec(lambda e, d=d, vb=vb: e.tensor_copy(
                t[vb][:, d * TC:(d + 1) * TC], t["sh"][:, 0:TC]), kind="vbf")
        ds = list(range(half * 4, half * 4 + 4))
        last = ln_norm(xb, ds, S["ln_bc"], consume_ln2)
        if half == 1:
            S["ev_vbf"][tc] = last
            if tc == 0:
                dbg_tap("v", vb, [("dve", S["ev_vbf"][tc])])

    def ph_gates(tc):
        par = tc % 2
        vb = "vbf%d" % par
        wsc = bank_wait(0) + [("dve", S["ev_vbf"][tc]), ("wq", wq_res)]
        fns = []
        for tb in range(TB):
            for d in range(DB):
                fns.append(lambda e, tb=tb, d=d, vb=vb: e.matmul(
                    t["p0"][:, tb * 8:(tb + 1) * 8],
                    t[vb][:, d * TC + tb * 128:d * TC + (tb + 1) * 128],
                    t["wgt"][:, d * 8:(d + 1) * 8],
                    start=(d == 0), stop=(d == DB - 1)))
        pe_sc = pe_group(fns, waits=wsc, kind="scores")
        ev_gsc = vec(lambda e: e.tensor_copy(t["gsc"][:, :], t["p0"][:, 0:32]),
                     waits=[("pe", pe_sc)], kind="g:gsc")
        bank_free[0] = ("dve", ev_gsc)
        ev_ge = act(lambda e: e.activation(t["ge"][:, :], t["gsc"][:, :],
                                           AF.Exp, bias=0.0, scale=1.0),
                    waits=[("dve", ev_gsc)], kind="g:exp")
        for tb in range(TB):
            vec(lambda e, tb=tb: e.max(out=t["m8"][:, tb * 8:(tb + 1) * 8],
                                       in_=t["gsc"][:, tb * 8:(tb + 1) * 8]),
                kind="g:max")
        for tb in range(TB):
            vec(lambda e, tb=tb: e.tensor_scalar(
                t["msk"][:, tb * 8:(tb + 1) * 8],
                t["gsc"][:, tb * 8:(tb + 1) * 8],
                t["m8"][:, tb * 8 + 1:tb * 8 + 2], None, OP.is_ge),
                kind="g:msk")
        vec(lambda e: e.tensor_mul(t["ge"][:, :], t["ge"][:, :],
                                   t["msk"][:, :]),
            waits=[("act", ev_ge)], kind="g:emask")
        for tb in range(TB):
            vec(lambda e, tb=tb: e.reduce_sum(
                t["zz"][:, tb:tb + 1], t["ge"][:, tb * 8:(tb + 1) * 8],
                axis=AX.X), kind="g:z")
        vec(lambda e: e.memset(t["junk"][:], 0.0), kind="g:gap")
        for tb in range(TB):
            vec(lambda e, tb=tb: e.reciprocal(t["rz"][:, tb:tb + 1],
                                              t["zz"][:, tb:tb + 1]),
                kind="g:rz")
            vec(lambda e: e.memset(t["junk"][:], 0.0), kind="g:gap")
        ev_gates = None
        for tb in range(TB):
            ev_gates = vec(lambda e, tb=tb: e.tensor_scalar_mul(
                t["gsc"][:, tb * 8:(tb + 1) * 8],
                t["ge"][:, tb * 8:(tb + 1) * 8],
                t["rz"][:, tb:tb + 1]), kind="g:gate")
        wtr = bank_wait(1) + [("dve", ev_gates), ("ld", ld_res)]
        fns = [(lambda e, tb=tb: e.transpose(
            t["p1"][0:8, tb * 128:(tb + 1) * 128],
            t["gsc"][:, tb * 8:(tb + 1) * 8], t["ident"][:, :]))
            for tb in range(TB)]
        pe_tr = pe_group(fns, waits=wtr, kind="g:tr")
        S["ev_gt"][tc] = vec(lambda e, par=par: e.tensor_copy(
            t["gt"][0:8, par * TC:(par + 1) * TC], t["p1"][0:8, 0:512]),
            waits=[("pe", pe_tr)], kind="g:gt")
        bank_free[1] = ("dve", S["ev_gt"][tc])
        if tc == 0:
            dbg_tap("gsc", "gsc", [("dve", ev_gates)])
            dbg_tap("gt", "gt", [("dve", S["ev_gt"][tc])])

    def make_units(tc):
        u = [lambda tc=tc: ph_ln1_stats(tc),
             lambda tc=tc: ph_ln1_norm(tc, 0),
             lambda tc=tc: ph_ln1_norm(tc, 1)]
        u += [lambda tc=tc, eb=eb: ph_dt_eb(tc, eb) for eb in range(DB)]
        u += [lambda tc=tc: ph_bx(tc)]
        u += [lambda tc=tc, n=n, p=p: ph_scan_pair(tc, n, p)
              for n in range(NST) for p in range(4)]
        u += [lambda tc=tc: ph_ln2_stats(tc),
              lambda tc=tc: ph_ln2_norm(tc, 0),
              lambda tc=tc: ph_ln2_norm(tc, 1),
              lambda tc=tc: ph_gates(tc)]
        return u

    def ffn_expert(tc, e_, tick):
        par = tc % 2
        xb = "x%d" % par
        vb = "vbf%d" % par
        epar = e_ % 2
        gb = "g%d" % epar
        ev_vbf = S["ev_vbf"][tc]
        d_a13 = stream(("a13", epar), lambda e, e_=e_, epar=epar:
            e.dma_start(out=t["a13r"][:, epar * DB * 64:(epar + 1) * DB * 64],
                        in_=dr["a13t"][e_]), kind="a13")
        d_a2 = stream(("a2", epar), lambda e, e_=e_, epar=epar:
            e.dma_start(out=t["a2r"][:, epar * HB * 32:(epar + 1) * HB * 32],
                        in_=dr["a2t"][e_]), kind="a2")
        d_b13 = stream(("b13", epar), lambda e, e_=e_, epar=epar:
            e.dma_start(out=t["b13r"][:, epar * HID:(epar + 1) * HID],
                        in_=dr["b13t"][e_]), kind="b13")
        d_b2 = stream(("b2", epar), lambda e, e_=e_, epar=epar:
            e.dma_start(out=t["b2r"][:, epar * D:(epar + 1) * D],
                        in_=dr["b2t"][e_]), kind="b2")

        def w13_stream(hb, e_=e_, tc=tc):
            slot = hb % W13_SLOTS
            wr_scr = False
            rd_scr = False
            if not rd_scr:
                dW = stream(("w13", slot),
                    lambda e, e_=e_, hb=hb, slot=slot: e.dma_start(
                        out=t["w13"][:, slot * 2048:(slot + 1) * 2048]
                        .rearrange("b (a c) -> b a c", a=2),
                        in_=dr["w13t"][e_, hb].transpose([1, 0, 2])),
                    kind="w13")
                if wr_scr:
                    sev = store(lambda e, e_=e_, hb=hb, slot=slot: e.dma_start(
                        out=dr["w13bf"][e_, hb],
                        in_=t["w13"][:, slot * 2048:(slot + 1) * 2048]),
                        waits=[("wq", dW)], kind="w13sc")
                    S["scr"][("w13", e_, hb)] = sev
                    S["scr_pend"][("w13", slot)] = ("st", sev)
            else:
                w = slot_rel.get(("w13", slot)) or []
                w = list(w) + [("st", S["scr"][("w13", e_, hb)])]
                dW = dma("sync", lambda e, e_=e_, hb=hb, slot=slot:
                    e.dma_start(
                        out=t["w13"][:, slot * 2048:(slot + 1) * 2048],
                        in_=dr["w13bf"][e_, hb]),
                    waits=w, kind="w13r")
            return (slot, dW)

        def w2_stream(db_, e_=e_, tc=tc):
            slot = db_ % W2_SLOTS
            wr_scr = False
            rd_scr = False
            if not rd_scr:
                dW = stream(("w2", slot),
                    lambda e, e_=e_, db_=db_, slot=slot: e.dma_start(
                        out=t["w2r"][:, slot * 2048:(slot + 1) * 2048],
                        in_=dr["w2t"][e_, db_]), kind="w2")
                if wr_scr:
                    sev = store(lambda e, e_=e_, db_=db_, slot=slot: e.dma_start(
                        out=dr["w2bf"][e_, db_],
                        in_=t["w2r"][:, slot * 2048:(slot + 1) * 2048]),
                        waits=[("wq", dW)], kind="w2sc")
                    S["scr"][("w2", e_, db_)] = sev
                    S["scr_pend"][("w2", slot)] = ("st", sev)
            else:
                w = slot_rel.get(("w2", slot)) or []
                w = list(w) + [("st", S["scr"][("w2", e_, db_)])]
                dW = dma("sync", lambda e, e_=e_, db_=db_, slot=slot:
                    e.dma_start(
                        out=t["w2r"][:, slot * 2048:(slot + 1) * 2048],
                        in_=dr["w2bf"][e_, db_]),
                    waits=w, kind="w2r")
            return (slot, dW)

        d_w13 = [w13_stream(hb) for hb in range(W13_SLOTS)]
        d_w2 = []

        wq13 = bank_wait(6) + [("wq", d_a13), ("dve", ev_vbf)]
        pq13 = pe_group([
            (lambda e, d=d, epar=epar, vb=vb: e.matmul(
                t["p6"][0:64, :],
                t["a13r"][:, epar * DB * 64 + d * 64: epar * DB * 64 + (d + 1) * 64],
                t[vb][:, d * TC:(d + 1) * TC],
                start=(d == 0), stop=(d == DB - 1)))
            for d in range(DB)], waits=wq13, kind="q13mm")
        slot_rel[("a13", epar)] = [("pe", pq13)]
        ev_q13 = act(lambda e, epar=epar: e.activation(
            t["q13"][0:64, epar * TC:(epar + 1) * TC], t["p6"][0:64, :],
            AF.Copy, bias=0.0, scale=1.0),
            waits=[("pe", pq13)], kind="q13evac")
        bank_free[6] = ("act", ev_q13)

        ev_g_wr = None
        for hb in range(HB):
            hpar = hb % 2
            b1k = 2 + hpar * 2
            b3k = 3 + hpar * 2
            slot, dW = d_w13[hb]
            wsem = "wq"
            w_h1 = bank_wait(b1k) + [(wsem, dW), ("dve", ev_vbf)]
            if hb == 0:
                w_h1 = w_h1 + [("wq", d_b13)]
            fns = [(lambda e, d=d, slot=slot, b1k=b1k, vb=vb: e.matmul(
                t["p%d" % b1k][:, :],
                t["w13"][:, slot * 2048 + d * 128: slot * 2048 + (d + 1) * 128],
                t[vb][:, d * TC:(d + 1) * TC],
                start=(d == 0), stop=False)) for d in range(DB)]
            p_h1 = pe_group(fns, waits=w_h1, kind="h1mm")
            w_h3 = bank_wait(b3k)
            fns = [(lambda e, d=d, slot=slot, b3k=b3k, vb=vb: e.matmul(
                t["p%d" % b3k][:, :],
                t["w13"][:, slot * 2048 + 1024 + d * 128:
                         slot * 2048 + 1024 + (d + 1) * 128],
                t[vb][:, d * TC:(d + 1) * TC],
                start=(d == 0), stop=False)) for d in range(DB)]
            fns.append(lambda e, hb=hb, epar=epar, b1k=b1k: e.matmul(
                t["p%d" % b1k][:, :],
                t["b13r"][0:32, epar * HID + hb * 128: epar * HID + (hb + 1) * 128],
                t["q13"][0:32, epar * TC:(epar + 1) * TC],
                start=False, stop=True))
            fns.append(lambda e, hb=hb, epar=epar, b3k=b3k: e.matmul(
                t["p%d" % b3k][:, :],
                t["b13r"][32:64, epar * HID + hb * 128: epar * HID + (hb + 1) * 128],
                t["q13"][32:64, epar * TC:(epar + 1) * TC],
                start=False, stop=True))
            p_h3 = pe_group(fns, waits=w_h3, kind="h3mm",
                            late={DB: [("act", ev_q13)]})
            p_h1 = p_h3
            if hb == 0:
                wgbc = bank_wait(7) + [("dve", S["ev_gt"][tc]), ("wq", wq_res)]
                pgbc = pe_group([lambda e, e_=e_, par=par: e.matmul(
                    t["p7"][:, :], t["sel8"][:, e_ * 128:(e_ + 1) * 128],
                    t["gt"][0:8, par * TC:(par + 1) * TC],
                    start=True, stop=True)], waits=wgbc, kind="gbcmm")
                ev_gbc = vec(lambda e, epar=epar: e.tensor_copy(
                    t["gbc"][:, epar * TC:(epar + 1) * TC], t["p7"][:, :]),
                    waits=[("pe", pgbc)], kind="gbcevac")
                bank_free[7] = ("dve", ev_gbc)
            rel = [("pe", p_h3)]
            if ("w13", slot) in S["scr_pend"]:
                rel.append(S["scr_pend"].pop(("w13", slot)))
            slot_rel[("w13", slot)] = rel
            if hb + W13_SLOTS < HB:
                d_w13.append(w13_stream(hb + W13_SLOTS))
            if hb >= HB - W2_SLOTS:
                d_w2.append(w2_stream(hb - (HB - W2_SLOTS)))
            if hb == HB - 1:
                slot_rel[("b13", epar)] = [("pe", p_h3)]
            w_si = [("pe", p_h1)]
            if last_gmul[hpar] is not None:
                w_si.append(("dve", last_gmul[hpar]))
            a_si = act(lambda e, hb=hb, e_=e_, hpar=hpar, b1k=b1k:
                e.activation(
                    t["silu"][:, hpar * TC:(hpar + 1) * TC],
                    t["p%d" % b1k][:, :], AF.Silu,
                    bias=t["b1r"][:, e_ * 16 + hb:e_ * 16 + hb + 1],
                    scale=1.0),
                waits=w_si, kind="silu")
            bank_free[b1k] = ("act", a_si)
            v_t1 = vec(lambda e, hb=hb, e_=e_, hpar=hpar, epar=epar, b3k=b3k:
                e.scalar_tensor_tensor(
                    t["t1"][:, hpar * TC:(hpar + 1) * TC],
                    t["p%d" % b3k][:, :],
                    t["b3r"][:, e_ * 16 + hb:e_ * 16 + hb + 1],
                    t["gbc"][:, epar * TC:(epar + 1) * TC],
                    OP.add, OP.mult),
                waits=[("pe", p_h3)], kind="t1stt")
            bank_free[b3k] = ("dve", v_t1)
            w_gm = [("act", a_si)]
            ev_g_wr = vec(lambda e, hb=hb, hpar=hpar, gb=gb: e.tensor_mul(
                t[gb][:, hb * TC:(hb + 1) * TC],
                t["t1"][:, hpar * TC:(hpar + 1) * TC],
                t["silu"][:, hpar * TC:(hpar + 1) * TC]),
                waits=w_gm, kind="gmul")
            last_gmul[hpar] = ev_g_wr
            tick("h")

        if tc == 0 and e_ == 0:
            dbg_tap("g", gb, [("dve", ev_g_wr)])
            dbg_tap("q13", "q13", [("act", ev_q13)])
            dbg_tap("gbc", "gbc", [("dve", ev_gbc)])
        wq2 = bank_wait(2) + [("wq", d_a2), ("dve", ev_g_wr)]
        pq2 = pe_group([
            (lambda e, hb=hb, epar=epar, gb=gb: e.matmul(
                t["p2"][0:32, :],
                t["a2r"][:, epar * HB * 32 + hb * 32: epar * HB * 32 + (hb + 1) * 32],
                t[gb][:, hb * TC:(hb + 1) * TC],
                start=(hb == 0), stop=(hb == HB - 1)))
            for hb in range(HB)], waits=wq2, kind="q2mm")
        slot_rel[("a2", epar)] = [("pe", pq2)]
        ev_q2 = act(lambda e, epar=epar: e.activation(
            t["q2"][0:32, epar * TC:(epar + 1) * TC], t["p2"][0:32, :],
            AF.Copy, bias=0.0, scale=1.0),
            waits=[("pe", pq2)], kind="q2evac")
        bank_free[2] = ("act", ev_q2)

        for db_ in range(DB):
            yk = 6 + (db_ % 2)
            slot, dW2 = d_w2[db_]
            wsem = "wq"
            wy = bank_wait(yk) + [(wsem, dW2), ("dve", ev_g_wr)]
            if db_ == 0:
                wy = wy + [("wq", d_b2)]
            fns = [(lambda e, hb=hb, slot=slot, yk=yk, gb=gb: e.matmul(
                t["p%d" % yk][:, :],
                t["w2r"][:, slot * 2048 + hb * 128: slot * 2048 + (hb + 1) * 128],
                t[gb][:, hb * TC:(hb + 1) * TC],
                start=(hb == 0), stop=False)) for hb in range(HB)]
            fns.append(lambda e, db_=db_, epar=epar, yk=yk: e.matmul(
                t["p%d" % yk][:, :],
                t["b2r"][0:32, epar * D + db_ * 128: epar * D + (db_ + 1) * 128],
                t["q2"][0:32, epar * TC:(epar + 1) * TC],
                start=False, stop=(e_ != 0)))
            if e_ == 0:
                fns.append(lambda e, db_=db_, par=par, yk=yk: e.matmul(
                    t["p%d" % yk][:, :],
                    t["b2v"][0:8, db_ * 128:(db_ + 1) * 128],
                    t["gt"][0:8, par * TC:(par + 1) * TC],
                    start=False, stop=True))
            p_y = pe_group(fns, waits=wy, kind="ymm",
                           late={HB: [("act", ev_q2)]})
            rel = [("pe", p_y)]
            if ("w2", slot) in S["scr_pend"]:
                rel.append(S["scr_pend"].pop(("w2", slot)))
            slot_rel[("w2", slot)] = rel
            if db_ + W2_SLOTS < DB:
                d_w2.append(w2_stream(db_ + W2_SLOTS))
            if db_ == DB - 1:
                slot_rel[("b2", epar)] = [("pe", p_y)]
            ev_y = vec(lambda e, db_=db_, xb=xb, yk=yk: e.tensor_add(
                t[xb][:, db_ * TC:(db_ + 1) * TC],
                t[xb][:, db_ * TC:(db_ + 1) * TC],
                t["p%d" % yk][:, :]),
                waits=[("pe", p_y)], kind="yevac")
            bank_free[yk] = ("dve", ev_y)
            tick("y")

    for ph in make_units(0):
        ph()
    prepop = []
    if NTC > 1:
        u1 = make_units(1)
        for ph in u1[:12]:
            ph()
        prepop = u1[12:]
    for tc in range(NTC):
        if tc == 0:
            pend = prepop
        else:
            pend = list(make_units(tc + 1)) if tc + 1 < NTC else []
        tickc = [0]

        def tick(kind, pend=pend, tickc=tickc):
            tickc[0] += 1
            if not pend:
                return
            if kind == "y" or tickc[0] % 4 == 0:
                pend.pop(0)()
        for e_ in range(E):
            ffn_expert(tc, e_, tick)
        while pend:
            pend.pop(0)()
        ev_ffn_done = cx.c("dve")
        if tc == 0:
            dbg_tap("y0", "x%d" % (tc % 2), [("dve", ev_ffn_done)])
        S["ev_store"][tc] = store(lambda e, tc=tc: e.dma_start(
            out=dr["out"][tc], in_=t["x%d" % (tc % 2)][:]),
            waits=[("dve", ev_ffn_done)])

    cx.waitonly("sync", [("st", cx.c("st"))])


def host_prep(i):
    p = {}
    f32 = np.float32

    def cc(a):
        return np.ascontiguousarray(a, dtype=f32)

    W1, W3, W2 = [np.asarray(i[k]) for k in
                  ("L0_ffn_W1", "L0_ffn_W3", "L0_ffn_W2")]
    w1t = W1.reshape(E, HB, 128, DB, 128).transpose(0, 1, 4, 3, 2)
    w3t = W3.reshape(E, HB, 128, DB, 128).transpose(0, 1, 4, 3, 2)
    w13 = np.stack([w1t, w3t], axis=2)
    p["w13t"] = cc(w13.reshape(E, HB, 2, 128, 1024))
    p["w2t"] = cc(W2.reshape(E, DB, 128, HB, 128)
                  .transpose(0, 1, 4, 3, 2).reshape(E, DB, 128, HB * 128))
    Wd = np.asarray(i["L0_attn_Wd"])
    p["wdt"] = cc(Wd.reshape(DB, 128, DB, 128).transpose(0, 3, 2, 1)
                  .reshape(DB, 128, DB * 128))
    Wb = np.asarray(i["L0_attn_Wb"])
    p["wbt"] = cc(Wb.reshape(NST, DB, 128).transpose(2, 1, 0).reshape(128, DB * NST))
    Wg = np.asarray(i["L0_ffn_Wg"])
    p["wgt"] = cc(Wg.reshape(E, DB, 128).transpose(2, 1, 0).reshape(128, DB * E))
    A1, B1 = np.asarray(i["L0_ffn_A1"]), np.asarray(i["L0_ffn_B1"])
    A3, B3 = np.asarray(i["L0_ffn_A3"]), np.asarray(i["L0_ffn_B3"])
    A2, B2 = np.asarray(i["L0_ffn_A2"]), np.asarray(i["L0_ffn_B2"])
    a13 = np.zeros((E, 64, D), f32)
    a13[:, 0:16] = A1
    a13[:, 32:48] = A3
    p["a13t"] = cc(a13.reshape(E, 64, DB, 128).transpose(0, 3, 2, 1)
                   .reshape(E, 128, DB * 64))
    b13 = np.zeros((E, 64, HID), f32)
    b13[:, 0:16] = B1.transpose(0, 2, 1)
    b13[:, 32:48] = B3.transpose(0, 2, 1)
    p["b13t"] = cc(b13)
    a2p = np.zeros((E, 32, HID), f32)
    a2p[:, 0:16] = A2
    p["a2t"] = cc(a2p.reshape(E, 32, HB, 128).transpose(0, 3, 2, 1)
                  .reshape(E, 128, HB * 32))
    b2p = np.zeros((E, 32, D), f32)
    b2p[:, 0:16] = B2.transpose(0, 2, 1)
    p["b2t"] = cc(b2p)
    p["b2v"] = cc(np.asarray(i["L0_ffn_b2"]))
    p["b1r"] = cc(np.asarray(i["L0_ffn_b1"]).reshape(E, HB, 128)
                  .transpose(2, 0, 1).reshape(128, E * HB))
    p["b3r"] = cc(np.asarray(i["L0_ffn_b3"]).reshape(E, HB, 128)
                  .transpose(2, 0, 1).reshape(128, E * HB))
    p["amat"] = cc(np.asarray(i["L0_attn_A"]).reshape(DB, 128, NST)
                   .transpose(1, 0, 2).reshape(128, DB * NST))
    p["cmat"] = cc(np.asarray(i["L0_attn_C"]).reshape(DB, 128, NST)
                   .transpose(1, 0, 2).reshape(128, DB * NST))
    p["dpm"] = cc(np.asarray(i["L0_attn_Dp"]).reshape(DB, 128).T)
    p["bdm"] = cc(np.asarray(i["L0_attn_bd"]).reshape(DB, 128).T)
    sel16 = np.zeros((16, NST * 128), f32)
    for n in range(NST):
        sel16[n, n * 128:(n + 1) * 128] = 1.0
    p["sel16"] = sel16
    sel8 = np.zeros((8, E * 128), f32)
    for n in range(E):
        sel8[n, n * 128:(n + 1) * 128] = 1.0
    p["sel8"] = sel8
    p["ident"] = np.eye(128, dtype=f32)
    return p


def kernel(**inputs):
    x = np.asarray(inputs["L0_attn_x"])
    shared = host_prep(inputs)
    nc = build_nc()
    in_maps = []
    for b in range(B):
        m = dict(shared)
        m["xin"] = np.ascontiguousarray(
            x[b].reshape(NTC, TC, DB, 128).transpose(0, 3, 2, 1)
            .reshape(NTC, 128, DB * TC))
        in_maps.append(m)
    res = run_bass_kernel_spmd(nc, in_maps, core_ids=list(range(B)))
    outs = []
    for b in range(B):
        o = res.results[b]["out"]
        o = o.reshape(NTC, 128, DB, TC).transpose(0, 3, 2, 1).reshape(L, D)
        outs.append(o)
    return np.stack(outs, axis=0).astype(np.float32)



# version 31
# speedup vs baseline: 1.0010x; 1.0010x over previous
import numpy as np
from contextlib import ExitStack

import concourse.bass as bass
import concourse.mybir as mybir
from concourse.bass_utils import run_bass_kernel_spmd

F32 = mybir.dt.float32
BF16 = mybir.dt.bfloat16
AF = mybir.ActivationFunctionType
OP = mybir.AluOpType
AX = mybir.AxisListType

B, L, D, NST, E, HID, R = 8, 2048, 1024, 16, 8, 2048, 16
TC = 512
NTC = L // TC
DB = D // 128
HB = HID // 128
TB = TC // 128
EPS = 1e-5
SERIAL = False

SEMS = ("ld", "st", "wq", "lw", "pe", "dve", "act", "gp")
SEM_ENG = {"pe": "tensor", "dve": "vector", "act": "scalar", "gp": "gpsimd"}

W13_SLOTS = 5
W2_SLOTS = 3
WDT_SLOTS = 2

DEBUG = ()


class Cx:
    def __init__(self, active, eng, sems, trace=None):
        self.active = active
        self.eng = eng
        self.sems = sems
        self.cnt = {s: 0 for s in SEMS}
        self.wmax = {}
        self.trace = trace

    def do(self, engname, fn, waits=(), inc=None, kind=""):
        waits = [(s, v) for (s, v) in waits
                 if v > 0 and SEM_ENG.get(s) != engname]
        pruned = []
        for s, v in waits:
            k = (engname, s)
            if v > self.wmax.get(k, 0):
                pruned.append((s, v))
                self.wmax[k] = v
        waits = pruned
        if SERIAL:
            extra = [(s, self.cnt[s]) for s in SEMS
                     if self.cnt[s] > 0 and SEM_ENG.get(s) != engname]
            seen = {s for s, _ in waits}
            waits = waits + [(s, v) for (s, v) in extra if s not in seen]
        if self.trace is not None:
            self.trace.append((engname, tuple(waits),
                               (inc,) if inc else (), kind))
        if self.active == engname:
            for s, v in waits:
                self.eng.wait_ge(self.sems[s], v)
            r = fn(self.eng)
            if inc is not None:
                r.then_inc(self.sems[inc[0]], inc[1])
        if inc is not None:
            self.cnt[inc[0]] += inc[1]
        return self.cnt[inc[0]] if inc else None

    def waitonly(self, engname, waits):
        waits = [(s, v) for (s, v) in waits if v > 0]
        pruned = []
        for s, v in waits:
            k = (engname, s)
            if v > self.wmax.get(k, 0):
                pruned.append((s, v))
                self.wmax[k] = v
        waits = pruned
        if self.trace is not None:
            self.trace.append((engname, tuple(waits), (), "wait"))
        if self.active == engname:
            for s, v in waits:
                self.eng.wait_ge(self.sems[s], v)

    def c(self, s):
        return self.cnt[s]


def _simulate(trace):
    engs = {}
    for op in trace:
        engs.setdefault(op[0], []).append(op)
    pcs = {e: 0 for e in engs}
    cnt = {s: 0 for s in SEMS}
    total = sum(len(v) for v in engs.values())
    done = 0
    while done < total:
        progressed = False
        for e, ops in engs.items():
            while pcs[e] < len(ops):
                _, waits, incs, kind = ops[pcs[e]]
                if any(cnt[s] < v for s, v in waits):
                    break
                for inc in incs:
                    cnt[inc[0]] += inc[1]
                pcs[e] += 1
                done += 1
                progressed = True
        if not progressed:
            lines = []
            for e, ops in engs.items():
                if pcs[e] < len(ops):
                    _, waits, _, kind = ops[pcs[e]]
                    blocked = [(s, v, cnt[s]) for s, v in waits if cnt[s] < v]
                    lines.append(f"{e} pc={pcs[e]} kind={kind} blocked on {blocked}")
            raise RuntimeError("DEADLOCK in schedule:\n" + "\n".join(lines))


def build_nc():
    nc = bass.Bass("TRN2", target_bir_lowering=False)

    dr = {}

    def din(name, shape, dt=F32):
        dr[name] = nc.dram_tensor(name, shape, dt, kind="ExternalInput").ap()
        return dr[name]

    din("xin", [NTC, 128, DB * TC])
    dr["out"] = nc.dram_tensor("out", [NTC, 128, DB * TC], F32,
                               kind="ExternalOutput").ap()
    din("w13t", [E, HB, 2, 128, 1024])
    din("w2t", [E, DB, 128, HB * 128])
    din("wdt", [DB, 128, DB * 128])
    din("wbt", [128, DB * NST])
    din("wgt", [128, DB * E])
    din("a13t", [E, 128, DB * 64])
    din("b13t", [E, 64, HID])
    din("a2t", [E, 128, HB * 32])
    din("b2t", [E, 32, D])
    din("b2v", [E, D])
    din("b1r", [128, E * HB])
    din("b3r", [128, E * HB])
    din("amat", [128, DB * NST])
    din("cmat", [128, DB * NST])
    din("dpm", [128, DB])
    din("bdm", [128, DB])
    din("sel16", [16, NST * 128])
    din("sel8", [8, E * 128])
    din("ident", [128, 128])
    dr["w13bf"] = nc.dram_tensor("w13bf", [E, HB, 128, 2048], BF16,
                                 kind="Internal").ap()
    dr["w2bf"] = nc.dram_tensor("w2bf", [E, DB, 128, 2048], BF16,
                                 kind="Internal").ap()
    _dbg_shapes = {"ubf": ([128, DB * TC], BF16), "dt": ([128, DB * TC], F32),
                   "bx": ([16, 2 * TC], BF16), "x2": ([128, DB * TC], F32),
                   "v": ([128, DB * TC], BF16), "gsc": ([128, 32], F32),
                   "gt": ([8, 2 * TC], BF16), "g": ([128, HB * TC], BF16),
                   "y0": ([128, DB * TC], F32), "q13": ([64, 2 * TC], BF16),
                   "gbc": ([128, 2 * TC], BF16)}
    for k in DEBUG:
        shp, dty = _dbg_shapes[k]
        dr["dbg_" + k] = nc.dram_tensor("dbg_" + k, shp, dty,
                                        kind="ExternalOutput").ap()

    with ExitStack() as ctx:
        t = {}

        def sb(name, shape, dt=F32):
            t[name] = ctx.enter_context(nc.sbuf_tensor("s_" + name, shape, dt))
            return t[name]

        def psum(name):
            t[name] = ctx.enter_context(nc.psum_tensor(name, [128, 512], F32))
            return t[name]

        sb("x0", [128, DB * TC]); sb("x1", [128, DB * TC])
        sb("ubf", [128, DB * TC], BF16)
        sb("vbf0", [128, DB * TC], BF16); sb("vbf1", [128, DB * TC], BF16)
        sb("dt", [128, DB * TC])
        sb("sq", [128, 2 * TC], BF16)
        sb("sa", [128, 4 * TC]); sb("sbb", [128, 2 * TC]); sb("sh", [128, 2 * TC])
        sb("carry", [128, 128])
        sb("bx", [16, 2 * TC], BF16)
        sb("g0", [128, HB * TC], BF16); sb("g1", [128, HB * TC], BF16)
        sb("silu", [128, 2 * TC], BF16)
        sb("t1", [128, 2 * TC], BF16)
        sb("q13", [64, 2 * TC], BF16)
        sb("q2", [32, 2 * TC], BF16)
        sb("gbc", [128, 2 * TC], BF16)
        sb("gt", [8, 2 * TC], BF16)
        sb("gsc", [128, 32]); sb("m8", [128, 32]); sb("msk", [128, 32])
        sb("ge", [128, 32]); sb("zz", [128, 4]); sb("rz", [128, 4])
        sb("junk", [128, 8])
        sb("stat", [1, 3 * TC])
        sb("epsr", [1, 1])
        sb("w13", [128, W13_SLOTS * 2048], BF16)
        sb("w2r", [128, W2_SLOTS * 2048], BF16)
        sb("wdtr", [128, WDT_SLOTS * 1024], BF16)
        sb("a13r", [128, 2 * DB * 64], BF16)
        sb("a2r", [128, 2 * HB * 32], BF16)
        sb("b13r", [64, 2 * HID], BF16)
        sb("b2r", [32, 2 * D], BF16)
        sb("wbt", [128, DB * NST], BF16)
        sb("wgt", [128, DB * E], BF16)

        sb("b2v", [E, D], BF16)
        sb("b1r", [128, E * HB]); sb("b3r", [128, E * HB])
        sb("amat", [128, DB * NST]); sb("cmat", [128, DB * NST])
        sb("dpm", [128, DB]); sb("bdm", [128, DB])
        sb("sel16", [16, NST * 128], BF16); sb("sel8", [8, E * 128], BF16)
        sb("ident", [128, 128])
        sb("onesc", [128, 1]); sb("onesb", [128, 1], BF16); sb("onesr", [1, 128])

        for i in range(8):
            psum("p%d" % i)

        sems = {s: ctx.enter_context(nc.semaphore(s)) for s in SEMS}
        block = ctx.enter_context(nc.Block())

        trace = []
        cx0 = Cx(None, None, sems, trace=trace)
        emit_all(cx0, t, dr)
        _simulate(trace)

        def mk(engname):
            def closure(eng):
                emit_all(Cx(engname, eng, sems), t, dr)
            return closure

        block.sync(mk("sync"))
        block.gpsimd(mk("gpsimd"))
        block.tensor(mk("tensor"))
        block.vector(mk("vector"))
        block.scalar(mk("scalar"))

    return nc


def emit_all(cx, t, dr):
    bank_free = {i: None for i in range(8)}

    def bank_wait(i):
        return [bank_free[i]] if bank_free[i] else []

    def dma(q, fn, waits=(), kind=""):
        sem = {"sync": "ld", "gpsimd": "wq", "scalar": "lw"}[q]
        cx.do(q, fn, waits=waits, inc=(sem, 16), kind=kind)
        return cx.c(sem)

    def store(fn, waits=(), kind="store"):
        cx.do("sync", fn, waits=waits, inc=("st", 16), kind=kind)
        return cx.c("st")

    def vec(fn, waits=(), kind=""):
        cx.do("vector", fn, waits=waits, inc=("dve", 1), kind=kind)
        return cx.c("dve")

    def act(fn, waits=(), kind=""):
        cx.do("scalar", fn, waits=waits, inc=("act", 1), kind=kind)
        return cx.c("act")

    def gp(fn, waits=(), kind=""):
        cx.do("gpsimd", fn, waits=waits, inc=("gp", 1), kind=kind)
        return cx.c("gp")

    def pe_group(fns, waits=(), kind="", late=None):
        for i, f in enumerate(fns):
            w = waits if i == 0 else ()
            if late and i in late:
                w = list(w) + late[i]
            cx.do("tensor", f, waits=w,
                  inc=("pe", 1) if i == len(fns) - 1 else None, kind=kind)
        return cx.c("pe")

    def dbg_tap(key, buf_name, waits):
        if key in DEBUG:
            store(lambda e: e.dma_start(out=dr["dbg_" + key],
                                        in_=t[buf_name][:]),
                  waits=waits, kind="dbg:" + key)

    res_names = ["b1r", "b3r", "amat", "cmat", "dpm", "bdm", "ident"]
    for nm in res_names:
        dma("sync", lambda e, nm=nm: e.dma_start(out=t[nm][:], in_=dr[nm][:]),
            kind="res:" + nm)
    ld_res = cx.c("ld")
    for nm in ["wbt", "wgt", "b2v", "sel16", "sel8"]:
        dma("gpsimd", lambda e, nm=nm: e.dma_start(out=t[nm][:], in_=dr[nm][:]),
            kind="resbf:" + nm)
    wq_res = cx.c("wq")

    vec(lambda e: e.memset(t["onesc"][:], 1.0), kind="ones")
    vec(lambda e: e.memset(t["onesb"][:], 1.0), kind="ones")
    vec(lambda e: e.memset(t["onesr"][:], 1.0), kind="ones")
    vec(lambda e: e.memset(t["junk"][:], 0.0), kind="ones")
    vec(lambda e: e.memset(t["carry"][:], 0.0), kind="ones")
    ev_ones = vec(lambda e: e.memset(t["epsr"][:], EPS), kind="ones")

    slot_rel = {}

    def stream(nm_slot, fn, kind, q="gpsimd"):
        w = slot_rel.get(nm_slot) or []
        if w and not isinstance(w, list):
            w = [w]
        return dma(q, fn, waits=w, kind=kind)

    def ln_stats(xb, dst_ev):
        S = t["stat"]
        mean_s = S[0:1, 0:TC]
        ms_s = S[0:1, TC:2 * TC]
        rstd_s = S[0:1, 2 * TC:3 * TC]
        w0 = bank_wait(0) + dst_ev + [("dve", ev_ones)]
        pe_mean = pe_group([
            (lambda e, d=d, xb=xb: e.matmul(
                t["p0"][0:1, :], t["onesc"][:, :],
                t[xb][:, d * TC:(d + 1) * TC],
                start=(d == 0), stop=(d == DB - 1)))
            for d in range(DB)], waits=w0, kind="ln:meanmm")
        w1 = bank_wait(1)
        pe_half = None
        for half in range(4):
            sq_ev = None
            for j in range(2):
                d = half * 2 + j
                wsq = list(dst_ev) if (half == 0 and j == 0) else []
                if half >= 1 and j == 0 and pe_half:
                    wsq.append(("pe", pe_half))
                sq_ev = act(lambda e, d=d, j=j, xb=xb: e.activation(
                    t["sq"][:, j * TC:(j + 1) * TC],
                    t[xb][:, d * TC:(d + 1) * TC],
                    AF.Square, bias=0.0, scale=1.0),
                    waits=wsq, kind="ln:sq")
            wmm = (w1 if half == 0 else []) + [("act", sq_ev),
                                               ("dve", ev_ones)]
            pe_half = pe_group([
                (lambda e, j=j, half=half: e.matmul(
                    t["p1"][0:1, :], t["onesb"][:, :],
                    t["sq"][:, j * TC:(j + 1) * TC],
                    start=(half == 0 and j == 0),
                    stop=(half == 3 and j == 1)))
                for j in range(2)], waits=wmm, kind="ln:sqmm")
        pe_sq = pe_half
        vec(lambda e: e.tensor_scalar_mul(mean_s, t["p0"][0:1, :], 1.0 / D),
            waits=[("pe", pe_mean)], kind="ln:mean")
        vec(lambda e: e.tensor_scalar_mul(ms_s, t["p1"][0:1, :], 1.0 / D),
            waits=[("pe", pe_sq)], kind="ln:ms")
        vec(lambda e: e.tensor_mul(rstd_s, mean_s, mean_s), kind="ln:m2")
        ev_var = vec(lambda e: e.tensor_sub(ms_s, ms_s, rstd_s), kind="ln:var")
        ev_shift = vec(lambda e: e.tensor_scalar_mul(mean_s, mean_s, -1.0),
                       kind="ln:shift")
        ev_sd = act(lambda e: e.activation(rstd_s, ms_s, AF.Sqrt,
                                           bias=t["epsr"][0:1, 0:1], scale=1.0),
                    waits=[("dve", ev_shift)], kind="ln:sd")
        ev_rstd = vec(lambda e: e.reciprocal(rstd_s, rstd_s),
                      waits=[("act", ev_sd)], kind="ln:rstd")
        pe_shift = pe_group([lambda e: e.matmul(
            t["p0"][:, :], t["onesr"][:, :], mean_s, start=True, stop=True)],
            waits=[("dve", ev_shift)], kind="ln:shiftbc")
        pe_rstd = pe_group([lambda e: e.matmul(
            t["p1"][:, :], t["onesr"][:, :], rstd_s, start=True, stop=True)],
            waits=[("dve", ev_rstd)], kind="ln:rstdbc")
        return (pe_shift, pe_rstd)

    def ln_norm(xb, ds, bc, consume):
        pe_shift, pe_rstd = bc
        usl = t["sh"][:, 0:TC]
        for d in ds:
            w = []
            if d == ds[0]:
                w = [("pe", pe_shift), ("pe", pe_rstd)]
                if S.get("carry_src") is not None:
                    w.append(S["carry_src"])
            vec(lambda e, d=d, xb=xb: e.tensor_add(
                usl, t[xb][:, d * TC:(d + 1) * TC], t["p0"][:, :]),
                waits=w, kind="ln:sub")
            vec(lambda e: e.tensor_mul(usl, usl, t["p1"][:, :]),
                kind="ln:mul")
            consume(d)
        last = cx.c("dve")
        if ds[-1] == DB - 1:
            bank_free[0] = ("dve", last)
            bank_free[1] = ("dve", last)
        return last

    S = dict(ev_dt=None, scan_guard=None, ev_store={}, ev_scan_done={},
             ev_vbf={}, ev_gt={}, ev_ubf={}, scan_hist={}, scr={},
             scr_pend={}, sh_free={0: None, 1: None}, carry_src=None)
    last_gmul = {0: None, 1: None}

    def sp1(p): return t["sa"][:, (2 * p) * TC:(2 * p + 1) * TC]
    def sp2(p): return t["sa"][:, (2 * p + 1) * TC:(2 * p + 2) * TC]
    def sp3(p): return t["sbb"][:, p * TC:(p + 1) * TC]

    def ph_ln1_stats(tc):
        par = tc % 2
        xb = "x%d" % par
        wx = [("st", S["ev_store"][tc - 2])] if tc >= 2 else []
        ev_xld = dma("sync", lambda e, tc=tc, xb=xb: e.dma_start(
            out=t[xb][:], in_=dr["xin"][tc]), waits=wx, kind="xload")
        S["ln_bc"] = ln_stats(xb, [("ld", ev_xld)])

    def ph_ln1_norm(tc, half):
        par = tc % 2
        xb = "x%d" % par

        def consume_ln1(d, xb=xb):
            vec(lambda e, d=d: e.tensor_copy(
                t["ubf"][:, d * TC:(d + 1) * TC], t["sh"][:, 0:TC]),
                kind="ubf")
            vec(lambda e, d=d, xb=xb: e.scalar_tensor_tensor(
                t[xb][:, d * TC:(d + 1) * TC], t["sh"][:, 0:TC],
                t["dpm"][:, d:d + 1],
                t[xb][:, d * TC:(d + 1) * TC], OP.mult, OP.add), kind="dp")
        ds = list(range(half * 4, half * 4 + 4))
        ln_norm(xb, ds, S["ln_bc"], consume_ln1)
        if half == 1:
            S["ev_ubf"][tc] = cx.c("dve")
            if tc == 0:
                dbg_tap("ubf", "ubf", [("dve", S["ev_ubf"][tc])])

    def ph_dt_eb(tc, eb):
        ev_ubf = S["ev_ubf"][tc]

        def wdt_stream(eb2):
            slot = eb2 % WDT_SLOTS
            return stream(("wdt", slot), lambda e, eb2=eb2, slot=slot:
                e.dma_start(out=t["wdtr"][:, slot * 1024:(slot + 1) * 1024],
                            in_=dr["wdt"][eb2]), kind="wdt")
        if eb == 0:
            S["wdt_dma"] = [wdt_stream(e2) for e2 in range(WDT_SLOTS)]
            S["ev_dt_par"] = {0: None, 1: None}
        wdt_dma = S["wdt_dma"]
        ev_dt_par = S["ev_dt_par"]
        if True:
            slot = eb % WDT_SLOTS
            bk = eb % 2
            pp = eb % 2
            wts = bank_wait(bk) + [("wq", wdt_dma[eb]), ("dve", ev_ubf)]
            pm = pe_group([
                (lambda e, d=d, slot=slot, bk=bk: e.matmul(
                    t["p%d" % bk][:, :],
                    t["wdtr"][:, slot * 1024 + d * 128: slot * 1024 + (d + 1) * 128],
                    t["ubf"][:, d * TC:(d + 1) * TC],
                    start=(d == 0), stop=(d == DB - 1)))
                for d in range(DB)], waits=wts, kind="dtmm")
            slot_rel[("wdt", slot)] = [("pe", pm)]
            if eb + WDT_SLOTS < DB:
                wdt_dma.append(wdt_stream(eb + WDT_SLOTS))
            wsp = [("pe", pm)]
            if eb < 2 and S["scan_guard"] is not None:
                wsp.append(("dve", S["scan_guard"]))
            elif ev_dt_par[pp] is not None:
                wsp.append(("dve", ev_dt_par[pp]))
            a1 = act(lambda e, eb=eb, bk=bk, pp=pp: e.activation(
                sp1(pp), t["p%d" % bk][:, :], AF.Relu,
                bias=t["bdm"][:, eb:eb + 1], scale=1.0),
                waits=wsp, kind="sp:relu")
            a2_ = act(lambda e, eb=eb, bk=bk, pp=pp: e.activation(
                sp2(pp), t["p%d" % bk][:, :], AF.Abs,
                bias=t["bdm"][:, eb:eb + 1], scale=1.0), kind="sp:abs")
            bank_free[bk] = ("act", a2_)
            act(lambda e, pp=pp: e.activation(sp3(pp), sp2(pp), AF.Exp,
                                              bias=0.0, scale=-1.0),
                kind="sp:exp")
            a4 = act(lambda e, pp=pp: e.activation(sp2(pp), sp3(pp), AF.Ln,
                                                   bias=1.0, scale=1.0),
                     kind="sp:ln")
            S["ev_dt"] = vec(lambda e, eb=eb, pp=pp: e.tensor_add(
                t["dt"][:, eb * TC:(eb + 1) * TC], sp1(pp), sp2(pp)),
                waits=[("act", a4)], kind="sp:add")
            ev_dt_par[pp] = S["ev_dt"]

    def ph_bx(tc):
        par = tc % 2
        ev_ubf = S["ev_ubf"][tc]
        if tc == 0:
            dbg_tap("dt", "dt", [("dve", S["ev_dt"])])
        wbx = bank_wait(0) + [("wq", wq_res), ("dve", ev_ubf)]
        pbx = pe_group([
            (lambda e, d=d: e.matmul(
                t["p0"][0:16, :], t["wbt"][:, d * 16:(d + 1) * 16],
                t["ubf"][:, d * TC:(d + 1) * TC],
                start=(d == 0), stop=(d == DB - 1)))
            for d in range(DB)], waits=wbx, kind="bxmm")
        S["ev_bx"] = vec(lambda e, par=par: e.tensor_copy(
            t["bx"][0:16, par * TC:(par + 1) * TC], t["p0"][0:16, :]),
            waits=[("pe", pbx)], kind="bxevac")
        bank_free[0] = ("dve", S["ev_bx"])

    def ph_scan_pair(tc, n, p):
        par = tc % 2
        xb = "x%d" % par
        bk = n % 2
        d0 = 2 * p
        d1 = d0 + 1
        if p == 0:
            wbc = bank_wait(bk) + [("dve", S["ev_bx"])]
            S["pbc"] = pe_group([lambda e, n=n, bk=bk, par=par: e.matmul(
                t["p%d" % bk][:, :], t["sel16"][:, n * 128:(n + 1) * 128],
                t["bx"][0:16, par * TC:(par + 1) * TC],
                start=True, stop=True)], waits=wbc, kind="bxbc")
        pbc = S["pbc"]
        u = n * 4 + p
        hp = u % 2
        asl = slice(hp * 2 * TC, (hp + 1) * 2 * TC)
        if u < 2:
            wexp = [("dve", S["ev_dt"])]
        else:
            wexp = [("dve", S["scan_hist"][u - 2])]
        for j, d in enumerate((d0, d1)):
            aev = act(lambda e, d=d, n=n, hp=hp, j=j: e.activation(
                t["sa"][:, (hp * 2 + j) * TC:(hp * 2 + j + 1) * TC],
                t["dt"][:, d * TC:(d + 1) * TC], AF.Exp,
                bias=0.0, scale=t["amat"][:, d * 16 + n:d * 16 + n + 1]),
                waits=wexp if j == 0 else (), kind="scan:exp")
        last_stt = None
        for j, d in enumerate((d0, d1)):
            last_stt = vec(lambda e, d=d, n=n, j=j, bk=bk:
                e.scalar_tensor_tensor(
                    t["sbb"][:, j * TC:(j + 1) * TC],
                    t["dt"][:, d * TC:(d + 1) * TC],
                    t["cmat"][:, d * 16 + n:d * 16 + n + 1],
                    t["p%d" % bk][:, :], OP.mult, OP.mult),
                waits=[("pe", pbc)] if (p == 0 and j == 0) else (),
                kind="scan:stt")
        if p == 3:
            bank_free[bk] = ("dve", last_stt)
        acol = t["sa"][:, (hp * 2 + 1) * TC:(hp * 2 + 1) * TC + 1]
        vec(lambda e, d1=d1, n=n: e.tensor_mul(
            t["junk"][:, 0:1], acol,
            t["carry"][:, d1 * 16 + n:d1 * 16 + n + 1]),
            waits=[("act", aev)], kind="scan:f1")
        vec(lambda e: e.memset(acol, 0.0), kind="scan:f3")
        vec(lambda e: e.tensor_add(
            t["sbb"][:, TC:TC + 1], t["sbb"][:, TC:TC + 1],
            t["junk"][:, 0:1]), kind="scan:f2")
        wsc = []
        if S["sh_free"][0] is not None:
            wsc.append(S["sh_free"][0])
        if S["carry_src"] is not None and u < DB:
            wsc.append(S["carry_src"])
        ev_scan = vec(lambda e, d0=d0, n=n, asl=asl: e.tensor_tensor_scan(
            t["sh"][:, 0:2 * TC], t["sa"][:, asl], t["sbb"][:, 0:2 * TC],
            t["carry"][:, d0 * 16 + n:d0 * 16 + n + 1],
            OP.mult, OP.add),
            waits=wsc, kind="scan:scan")
        S["scan_hist"][u] = ev_scan
        vec(lambda e, d0=d0: e.tensor_add(
            t[xb][:, d0 * TC:(d0 + 2) * TC],
            t[xb][:, d0 * TC:(d0 + 2) * TC], t["sh"][:, 0:2 * TC]),
            kind="scan:add")
        cev = vec(lambda e, d0=d0, d1=d1, n=n: e.tensor_copy(
            bass.AP(t["carry"], d0 * 16 + n, [[128, 128], [16, 2]]),
            bass.AP(t["sh"], TC - 1, [[2 * TC, 128], [TC, 2]])),
            kind="scan:carry")
        S["scan_guard"] = ev_scan
        S["sh_free"][0] = None
        if n == NST - 1 and p == 3:
            done = [("dve", cx.c("dve"))]
            S["carry_src"] = ("dve", cx.c("dve"))
            S["ev_scan_done"][tc] = done
            if tc == 0:
                dbg_tap("x2", xb, done)

    def ph_ln2_stats(tc):
        xb = "x%d" % (tc % 2)
        S["ln_bc"] = ln_stats(xb, list(S["ev_scan_done"][tc]))

    def ph_ln2_norm(tc, half):
        par = tc % 2
        xb = "x%d" % par
        vb = "vbf%d" % par

        def consume_ln2(d, vb=vb):
            vec(lambda e, d=d, vb=vb: e.tensor_copy(
                t[vb][:, d * TC:(d + 1) * TC], t["sh"][:, 0:TC]), kind="vbf")
        ds = list(range(half * 4, half * 4 + 4))
        last = ln_norm(xb, ds, S["ln_bc"], consume_ln2)
        if half == 1:
            S["ev_vbf"][tc] = last
            if tc == 0:
                dbg_tap("v", vb, [("dve", S["ev_vbf"][tc])])

    def ph_gates(tc):
        par = tc % 2
        vb = "vbf%d" % par
        wsc = bank_wait(0) + [("dve", S["ev_vbf"][tc]), ("wq", wq_res)]
        fns = []
        for tb in range(TB):
            for d in range(DB):
                fns.append(lambda e, tb=tb, d=d, vb=vb: e.matmul(
                    t["p0"][:, tb * 8:(tb + 1) * 8],
                    t[vb][:, d * TC + tb * 128:d * TC + (tb + 1) * 128],
                    t["wgt"][:, d * 8:(d + 1) * 8],
                    start=(d == 0), stop=(d == DB - 1)))
        pe_sc = pe_group(fns, waits=wsc, kind="scores")
        ev_gsc = vec(lambda e: e.tensor_copy(t["gsc"][:, :], t["p0"][:, 0:32]),
                     waits=[("pe", pe_sc)], kind="g:gsc")
        bank_free[0] = ("dve", ev_gsc)
        ev_ge = act(lambda e: e.activation(t["ge"][:, :], t["gsc"][:, :],
                                           AF.Exp, bias=0.0, scale=1.0),
                    waits=[("dve", ev_gsc)], kind="g:exp")
        for tb in range(TB):
            vec(lambda e, tb=tb: e.max(out=t["m8"][:, tb * 8:(tb + 1) * 8],
                                       in_=t["gsc"][:, tb * 8:(tb + 1) * 8]),
                kind="g:max")
        for tb in range(TB):
            vec(lambda e, tb=tb: e.tensor_scalar(
                t["msk"][:, tb * 8:(tb + 1) * 8],
                t["gsc"][:, tb * 8:(tb + 1) * 8],
                t["m8"][:, tb * 8 + 1:tb * 8 + 2], None, OP.is_ge),
                kind="g:msk")
        vec(lambda e: e.tensor_mul(t["ge"][:, :], t["ge"][:, :],
                                   t["msk"][:, :]),
            waits=[("act", ev_ge)], kind="g:emask")
        for tb in range(TB):
            vec(lambda e, tb=tb: e.reduce_sum(
                t["zz"][:, tb:tb + 1], t["ge"][:, tb * 8:(tb + 1) * 8],
                axis=AX.X), kind="g:z")
        vec(lambda e: e.memset(t["junk"][:], 0.0), kind="g:gap")
        for tb in range(TB):
            vec(lambda e, tb=tb: e.reciprocal(t["rz"][:, tb:tb + 1],
                                              t["zz"][:, tb:tb + 1]),
                kind="g:rz")
            vec(lambda e: e.memset(t["junk"][:], 0.0), kind="g:gap")
        ev_gates = None
        for tb in range(TB):
            ev_gates = vec(lambda e, tb=tb: e.tensor_scalar_mul(
                t["gsc"][:, tb * 8:(tb + 1) * 8],
                t["ge"][:, tb * 8:(tb + 1) * 8],
                t["rz"][:, tb:tb + 1]), kind="g:gate")
        wtr = bank_wait(1) + [("dve", ev_gates), ("ld", ld_res)]
        fns = [(lambda e, tb=tb: e.transpose(
            t["p1"][0:8, tb * 128:(tb + 1) * 128],
            t["gsc"][:, tb * 8:(tb + 1) * 8], t["ident"][:, :]))
            for tb in range(TB)]
        pe_tr = pe_group(fns, waits=wtr, kind="g:tr")
        S["ev_gt"][tc] = vec(lambda e, par=par: e.tensor_copy(
            t["gt"][0:8, par * TC:(par + 1) * TC], t["p1"][0:8, 0:512]),
            waits=[("pe", pe_tr)], kind="g:gt")
        bank_free[1] = ("dve", S["ev_gt"][tc])
        if tc == 0:
            dbg_tap("gsc", "gsc", [("dve", ev_gates)])
            dbg_tap("gt", "gt", [("dve", S["ev_gt"][tc])])

    def make_units(tc):
        u = [lambda tc=tc: ph_ln1_stats(tc),
             lambda tc=tc: ph_ln1_norm(tc, 0),
             lambda tc=tc: ph_ln1_norm(tc, 1)]
        u += [lambda tc=tc, eb=eb: ph_dt_eb(tc, eb) for eb in range(DB)]
        u += [lambda tc=tc: ph_bx(tc)]
        u += [lambda tc=tc, n=n, p=p: ph_scan_pair(tc, n, p)
              for n in range(NST) for p in range(4)]
        u += [lambda tc=tc: ph_ln2_stats(tc),
              lambda tc=tc: ph_ln2_norm(tc, 0),
              lambda tc=tc: ph_ln2_norm(tc, 1),
              lambda tc=tc: ph_gates(tc)]
        return u

    def ffn_expert(tc, e_, tick):
        par = tc % 2
        xb = "x%d" % par
        vb = "vbf%d" % par
        epar = e_ % 2
        gb = "g%d" % epar
        ev_vbf = S["ev_vbf"][tc]
        d_a13 = stream(("a13", epar), lambda e, e_=e_, epar=epar:
            e.dma_start(out=t["a13r"][:, epar * DB * 64:(epar + 1) * DB * 64],
                        in_=dr["a13t"][e_]), kind="a13")
        d_a2 = stream(("a2", epar), lambda e, e_=e_, epar=epar:
            e.dma_start(out=t["a2r"][:, epar * HB * 32:(epar + 1) * HB * 32],
                        in_=dr["a2t"][e_]), kind="a2")
        d_b13 = stream(("b13", epar), lambda e, e_=e_, epar=epar:
            e.dma_start(out=t["b13r"][:, epar * HID:(epar + 1) * HID],
                        in_=dr["b13t"][e_]), kind="b13")
        d_b2 = stream(("b2", epar), lambda e, e_=e_, epar=epar:
            e.dma_start(out=t["b2r"][:, epar * D:(epar + 1) * D],
                        in_=dr["b2t"][e_]), kind="b2")

        def w13_stream(hb, e_=e_, tc=tc):
            slot = hb % W13_SLOTS
            wr_scr = False
            rd_scr = False
            if not rd_scr:
                dW = stream(("w13", slot),
                    lambda e, e_=e_, hb=hb, slot=slot: e.dma_start(
                        out=t["w13"][:, slot * 2048:(slot + 1) * 2048]
                        .rearrange("b (a c) -> b a c", a=2),
                        in_=dr["w13t"][e_, hb].transpose([1, 0, 2])),
                    kind="w13")
                if wr_scr:
                    sev = store(lambda e, e_=e_, hb=hb, slot=slot: e.dma_start(
                        out=dr["w13bf"][e_, hb],
                        in_=t["w13"][:, slot * 2048:(slot + 1) * 2048]),
                        waits=[("wq", dW)], kind="w13sc")
                    S["scr"][("w13", e_, hb)] = sev
                    S["scr_pend"][("w13", slot)] = ("st", sev)
            else:
                w = slot_rel.get(("w13", slot)) or []
                w = list(w) + [("st", S["scr"][("w13", e_, hb)])]
                dW = dma("sync", lambda e, e_=e_, hb=hb, slot=slot:
                    e.dma_start(
                        out=t["w13"][:, slot * 2048:(slot + 1) * 2048],
                        in_=dr["w13bf"][e_, hb]),
                    waits=w, kind="w13r")
            return (slot, dW)

        def w2_stream(db_, e_=e_, tc=tc):
            slot = db_ % W2_SLOTS
            wr_scr = False
            rd_scr = False
            if not rd_scr:
                dW = stream(("w2", slot),
                    lambda e, e_=e_, db_=db_, slot=slot: e.dma_start(
                        out=t["w2r"][:, slot * 2048:(slot + 1) * 2048],
                        in_=dr["w2t"][e_, db_]), kind="w2")
                if wr_scr:
                    sev = store(lambda e, e_=e_, db_=db_, slot=slot: e.dma_start(
                        out=dr["w2bf"][e_, db_],
                        in_=t["w2r"][:, slot * 2048:(slot + 1) * 2048]),
                        waits=[("wq", dW)], kind="w2sc")
                    S["scr"][("w2", e_, db_)] = sev
                    S["scr_pend"][("w2", slot)] = ("st", sev)
            else:
                w = slot_rel.get(("w2", slot)) or []
                w = list(w) + [("st", S["scr"][("w2", e_, db_)])]
                dW = dma("sync", lambda e, e_=e_, db_=db_, slot=slot:
                    e.dma_start(
                        out=t["w2r"][:, slot * 2048:(slot + 1) * 2048],
                        in_=dr["w2bf"][e_, db_]),
                    waits=w, kind="w2r")
            return (slot, dW)

        d_w13 = [w13_stream(hb) for hb in range(W13_SLOTS)]
        d_w2 = []

        wq13 = bank_wait(6) + [("wq", d_a13), ("dve", ev_vbf)]
        pq13 = pe_group([
            (lambda e, d=d, epar=epar, vb=vb: e.matmul(
                t["p6"][0:64, :],
                t["a13r"][:, epar * DB * 64 + d * 64: epar * DB * 64 + (d + 1) * 64],
                t[vb][:, d * TC:(d + 1) * TC],
                start=(d == 0), stop=(d == DB - 1)))
            for d in range(DB)], waits=wq13, kind="q13mm")
        slot_rel[("a13", epar)] = [("pe", pq13)]
        ev_q13 = act(lambda e, epar=epar: e.activation(
            t["q13"][0:64, epar * TC:(epar + 1) * TC], t["p6"][0:64, :],
            AF.Copy, bias=0.0, scale=1.0),
            waits=[("pe", pq13)], kind="q13evac")
        bank_free[6] = ("act", ev_q13)

        ev_g_wr = None
        for hb in range(HB):
            hpar = hb % 2
            b1k = 2 + hpar * 2
            b3k = 3 + hpar * 2
            slot, dW = d_w13[hb]
            wsem = "wq"
            w_h1 = bank_wait(b1k) + [(wsem, dW), ("dve", ev_vbf)]
            if hb == 0:
                w_h1 = w_h1 + [("wq", d_b13)]
            fns = [(lambda e, d=d, slot=slot, b1k=b1k, vb=vb: e.matmul(
                t["p%d" % b1k][:, :],
                t["w13"][:, slot * 2048 + d * 128: slot * 2048 + (d + 1) * 128],
                t[vb][:, d * TC:(d + 1) * TC],
                start=(d == 0), stop=False)) for d in range(DB)]
            p_h1 = pe_group(fns, waits=w_h1, kind="h1mm")
            w_h3 = bank_wait(b3k)
            fns = [(lambda e, d=d, slot=slot, b3k=b3k, vb=vb: e.matmul(
                t["p%d" % b3k][:, :],
                t["w13"][:, slot * 2048 + 1024 + d * 128:
                         slot * 2048 + 1024 + (d + 1) * 128],
                t[vb][:, d * TC:(d + 1) * TC],
                start=(d == 0), stop=False)) for d in range(DB)]
            fns.append(lambda e, hb=hb, epar=epar, b1k=b1k: e.matmul(
                t["p%d" % b1k][:, :],
                t["b13r"][0:32, epar * HID + hb * 128: epar * HID + (hb + 1) * 128],
                t["q13"][0:32, epar * TC:(epar + 1) * TC],
                start=False, stop=True))
            fns.append(lambda e, hb=hb, epar=epar, b3k=b3k: e.matmul(
                t["p%d" % b3k][:, :],
                t["b13r"][32:64, epar * HID + hb * 128: epar * HID + (hb + 1) * 128],
                t["q13"][32:64, epar * TC:(epar + 1) * TC],
                start=False, stop=True))
            p_h3 = pe_group(fns, waits=w_h3, kind="h3mm",
                            late={DB: [("act", ev_q13)]})
            p_h1 = p_h3
            if hb == 0:
                wgbc = bank_wait(7) + [("dve", S["ev_gt"][tc]), ("wq", wq_res)]
                pgbc = pe_group([lambda e, e_=e_, par=par: e.matmul(
                    t["p7"][:, :], t["sel8"][:, e_ * 128:(e_ + 1) * 128],
                    t["gt"][0:8, par * TC:(par + 1) * TC],
                    start=True, stop=True)], waits=wgbc, kind="gbcmm")
                ev_gbc = vec(lambda e, epar=epar: e.tensor_copy(
                    t["gbc"][:, epar * TC:(epar + 1) * TC], t["p7"][:, :]),
                    waits=[("pe", pgbc)], kind="gbcevac")
                bank_free[7] = ("dve", ev_gbc)
            rel = [("pe", p_h3)]
            if ("w13", slot) in S["scr_pend"]:
                rel.append(S["scr_pend"].pop(("w13", slot)))
            slot_rel[("w13", slot)] = rel
            if hb + W13_SLOTS < HB:
                d_w13.append(w13_stream(hb + W13_SLOTS))
            if hb >= HB - W2_SLOTS:
                d_w2.append(w2_stream(hb - (HB - W2_SLOTS)))
            if hb == HB - 1:
                slot_rel[("b13", epar)] = [("pe", p_h3)]
            w_si = [("pe", p_h1)]
            if last_gmul[hpar] is not None:
                w_si.append(("dve", last_gmul[hpar]))
            a_si = act(lambda e, hb=hb, e_=e_, hpar=hpar, b1k=b1k:
                e.activation(
                    t["silu"][:, hpar * TC:(hpar + 1) * TC],
                    t["p%d" % b1k][:, :], AF.Silu,
                    bias=t["b1r"][:, e_ * 16 + hb:e_ * 16 + hb + 1],
                    scale=1.0),
                waits=w_si, kind="silu")
            bank_free[b1k] = ("act", a_si)
            v_t1 = vec(lambda e, hb=hb, e_=e_, hpar=hpar, epar=epar, b3k=b3k:
                e.scalar_tensor_tensor(
                    t["t1"][:, hpar * TC:(hpar + 1) * TC],
                    t["p%d" % b3k][:, :],
                    t["b3r"][:, e_ * 16 + hb:e_ * 16 + hb + 1],
                    t["gbc"][:, epar * TC:(epar + 1) * TC],
                    OP.add, OP.mult),
                waits=[("pe", p_h3)], kind="t1stt")
            bank_free[b3k] = ("dve", v_t1)
            w_gm = [("act", a_si)]
            ev_g_wr = vec(lambda e, hb=hb, hpar=hpar, gb=gb: e.tensor_mul(
                t[gb][:, hb * TC:(hb + 1) * TC],
                t["t1"][:, hpar * TC:(hpar + 1) * TC],
                t["silu"][:, hpar * TC:(hpar + 1) * TC]),
                waits=w_gm, kind="gmul")
            last_gmul[hpar] = ev_g_wr
            tick("h")

        if tc == 0 and e_ == 0:
            dbg_tap("g", gb, [("dve", ev_g_wr)])
            dbg_tap("q13", "q13", [("act", ev_q13)])
            dbg_tap("gbc", "gbc", [("dve", ev_gbc)])
        wq2 = bank_wait(2) + [("wq", d_a2), ("dve", ev_g_wr)]
        pq2 = pe_group([
            (lambda e, hb=hb, epar=epar, gb=gb: e.matmul(
                t["p2"][0:32, :],
                t["a2r"][:, epar * HB * 32 + hb * 32: epar * HB * 32 + (hb + 1) * 32],
                t[gb][:, hb * TC:(hb + 1) * TC],
                start=(hb == 0), stop=(hb == HB - 1)))
            for hb in range(HB)], waits=wq2, kind="q2mm")
        slot_rel[("a2", epar)] = [("pe", pq2)]
        ev_q2 = act(lambda e, epar=epar: e.activation(
            t["q2"][0:32, epar * TC:(epar + 1) * TC], t["p2"][0:32, :],
            AF.Copy, bias=0.0, scale=1.0),
            waits=[("pe", pq2)], kind="q2evac")
        bank_free[2] = ("act", ev_q2)

        for db_ in range(DB):
            yk = 6 + (db_ % 2)
            slot, dW2 = d_w2[db_]
            wsem = "wq"
            wy = bank_wait(yk) + [(wsem, dW2), ("dve", ev_g_wr)]
            if db_ == 0:
                wy = wy + [("wq", d_b2)]
            fns = [(lambda e, hb=hb, slot=slot, yk=yk, gb=gb: e.matmul(
                t["p%d" % yk][:, :],
                t["w2r"][:, slot * 2048 + hb * 128: slot * 2048 + (hb + 1) * 128],
                t[gb][:, hb * TC:(hb + 1) * TC],
                start=(hb == 0), stop=False)) for hb in range(HB)]
            fns.append(lambda e, db_=db_, epar=epar, yk=yk: e.matmul(
                t["p%d" % yk][:, :],
                t["b2r"][0:32, epar * D + db_ * 128: epar * D + (db_ + 1) * 128],
                t["q2"][0:32, epar * TC:(epar + 1) * TC],
                start=False, stop=(e_ != 0)))
            if e_ == 0:
                fns.append(lambda e, db_=db_, par=par, yk=yk: e.matmul(
                    t["p%d" % yk][:, :],
                    t["b2v"][0:8, db_ * 128:(db_ + 1) * 128],
                    t["gt"][0:8, par * TC:(par + 1) * TC],
                    start=False, stop=True))
            p_y = pe_group(fns, waits=wy, kind="ymm",
                           late={HB: [("act", ev_q2)]})
            rel = [("pe", p_y)]
            if ("w2", slot) in S["scr_pend"]:
                rel.append(S["scr_pend"].pop(("w2", slot)))
            slot_rel[("w2", slot)] = rel
            if db_ + W2_SLOTS < DB:
                d_w2.append(w2_stream(db_ + W2_SLOTS))
            if db_ == DB - 1:
                slot_rel[("b2", epar)] = [("pe", p_y)]
            ev_y = vec(lambda e, db_=db_, xb=xb, yk=yk: e.tensor_add(
                t[xb][:, db_ * TC:(db_ + 1) * TC],
                t[xb][:, db_ * TC:(db_ + 1) * TC],
                t["p%d" % yk][:, :]),
                waits=[("pe", p_y)], kind="yevac")
            bank_free[yk] = ("dve", ev_y)
            tick("y")

    for ph in make_units(0):
        ph()
    prepop = []
    if NTC > 1:
        u1 = make_units(1)
        for ph in u1[:12]:
            ph()
        prepop = u1[12:]
    for tc in range(NTC):
        if tc == 0:
            pend = prepop
        else:
            pend = list(make_units(tc + 1)) if tc + 1 < NTC else []
        tickc = [0]

        def tick(kind, pend=pend, tickc=tickc):
            tickc[0] += 1
            if not pend:
                return
            if kind == "y" or tickc[0] % 4 == 0:
                pend.pop(0)()
        for e_ in range(E):
            ffn_expert(tc, e_, tick)
        while pend:
            pend.pop(0)()
        ev_ffn_done = cx.c("dve")
        if tc == 0:
            dbg_tap("y0", "x%d" % (tc % 2), [("dve", ev_ffn_done)])
        S["ev_store"][tc] = store(lambda e, tc=tc: e.dma_start(
            out=dr["out"][tc], in_=t["x%d" % (tc % 2)][:]),
            waits=[("dve", ev_ffn_done)])

    cx.waitonly("sync", [("st", cx.c("st"))])


def host_prep(i):
    p = {}
    f32 = np.float32

    def cc(a):
        return np.ascontiguousarray(a, dtype=f32)

    W1, W3, W2 = [np.asarray(i[k]) for k in
                  ("L0_ffn_W1", "L0_ffn_W3", "L0_ffn_W2")]
    w1t = W1.reshape(E, HB, 128, DB, 128).transpose(0, 1, 4, 3, 2)
    w3t = W3.reshape(E, HB, 128, DB, 128).transpose(0, 1, 4, 3, 2)
    w13 = np.stack([w1t, w3t], axis=2)
    p["w13t"] = cc(w13.reshape(E, HB, 2, 128, 1024))
    p["w2t"] = cc(W2.reshape(E, DB, 128, HB, 128)
                  .transpose(0, 1, 4, 3, 2).reshape(E, DB, 128, HB * 128))
    Wd = np.asarray(i["L0_attn_Wd"])
    p["wdt"] = cc(Wd.reshape(DB, 128, DB, 128).transpose(0, 3, 2, 1)
                  .reshape(DB, 128, DB * 128))
    Wb = np.asarray(i["L0_attn_Wb"])
    p["wbt"] = cc(Wb.reshape(NST, DB, 128).transpose(2, 1, 0).reshape(128, DB * NST))
    Wg = np.asarray(i["L0_ffn_Wg"])
    p["wgt"] = cc(Wg.reshape(E, DB, 128).transpose(2, 1, 0).reshape(128, DB * E))
    A1, B1 = np.asarray(i["L0_ffn_A1"]), np.asarray(i["L0_ffn_B1"])
    A3, B3 = np.asarray(i["L0_ffn_A3"]), np.asarray(i["L0_ffn_B3"])
    A2, B2 = np.asarray(i["L0_ffn_A2"]), np.asarray(i["L0_ffn_B2"])
    a13 = np.zeros((E, 64, D), f32)
    a13[:, 0:16] = A1
    a13[:, 32:48] = A3
    p["a13t"] = cc(a13.reshape(E, 64, DB, 128).transpose(0, 3, 2, 1)
                   .reshape(E, 128, DB * 64))
    b13 = np.zeros((E, 64, HID), f32)
    b13[:, 0:16] = B1.transpose(0, 2, 1)
    b13[:, 32:48] = B3.transpose(0, 2, 1)
    p["b13t"] = cc(b13)
    a2p = np.zeros((E, 32, HID), f32)
    a2p[:, 0:16] = A2
    p["a2t"] = cc(a2p.reshape(E, 32, HB, 128).transpose(0, 3, 2, 1)
                  .reshape(E, 128, HB * 32))
    b2p = np.zeros((E, 32, D), f32)
    b2p[:, 0:16] = B2.transpose(0, 2, 1)
    p["b2t"] = cc(b2p)
    p["b2v"] = cc(np.asarray(i["L0_ffn_b2"]))
    p["b1r"] = cc(np.asarray(i["L0_ffn_b1"]).reshape(E, HB, 128)
                  .transpose(2, 0, 1).reshape(128, E * HB))
    p["b3r"] = cc(np.asarray(i["L0_ffn_b3"]).reshape(E, HB, 128)
                  .transpose(2, 0, 1).reshape(128, E * HB))
    p["amat"] = cc(np.asarray(i["L0_attn_A"]).reshape(DB, 128, NST)
                   .transpose(1, 0, 2).reshape(128, DB * NST))
    p["cmat"] = cc(np.asarray(i["L0_attn_C"]).reshape(DB, 128, NST)
                   .transpose(1, 0, 2).reshape(128, DB * NST))
    p["dpm"] = cc(np.asarray(i["L0_attn_Dp"]).reshape(DB, 128).T)
    p["bdm"] = cc(np.asarray(i["L0_attn_bd"]).reshape(DB, 128).T)
    sel16 = np.zeros((16, NST * 128), f32)
    for n in range(NST):
        sel16[n, n * 128:(n + 1) * 128] = 1.0
    p["sel16"] = sel16
    sel8 = np.zeros((8, E * 128), f32)
    for n in range(E):
        sel8[n, n * 128:(n + 1) * 128] = 1.0
    p["sel8"] = sel8
    p["ident"] = np.eye(128, dtype=f32)
    return p


def kernel(**inputs):
    x = np.asarray(inputs["L0_attn_x"])
    shared = host_prep(inputs)
    nc = build_nc()
    in_maps = []
    for b in range(B):
        m = dict(shared)
        m["xin"] = np.ascontiguousarray(
            x[b].reshape(NTC, TC, DB, 128).transpose(0, 3, 2, 1)
            .reshape(NTC, 128, DB * TC))
        in_maps.append(m)
    res = run_bass_kernel_spmd(nc, in_maps, core_ids=list(range(B)))
    outs = []
    for b in range(B):
        o = res.results[b]["out"]
        o = o.reshape(NTC, 128, DB, TC).transpose(0, 3, 2, 1).reshape(L, D)
        outs.append(o)
    return np.stack(outs, axis=0).astype(np.float32)

